# version 04
# baseline (speedup 1.0000x reference)
import contextlib
import numpy as np
import ml_dtypes
import concourse.bass as bass
import concourse.mybir as mybir
from concourse.bass_utils import run_bass_kernel_spmd

F32 = mybir.dt.float32
BF16 = mybir.dt.bfloat16
I32 = mybir.dt.int32
AF = mybir.ActivationFunctionType
OP = mybir.AluOpType
AX = mybir.AxisListType

NCORES = 8
B, S, D = 2, 4096, 4096
TOK = 1024
NT = TOK // 128
KT = D // 128
HFF = 11008
NHT = HFF // 128
H, KVH, HD = 32, 8, 128
EPS = 1e-5

BF = ml_dtypes.bfloat16



class Ser:

    def __init__(self, eng, sem):
        self.eng, self.sem, self.n = eng, sem, 0

    def __call__(self, instr):
        self.n += 1
        instr.then_inc(self.sem, 1)
        self.eng.wait_ge(self.sem, self.n)
        return instr




def _mkctx(nc, block, es, dram, sd, ph, prefix):
    return dict(nc=nc, block=block, es=es, dram=dram, sd=sd, ph=ph, prefix=prefix)


def _standalone(emit, dram_specs, outputs, name):
    nc = bass.Bass("TRN2", target_bir_lowering=False)
    dram = {}
    for nm, shape, dt, kind in dram_specs:
        dram[nm] = nc.dram_tensor(nm, shape, dt, kind=kind)
    with contextlib.ExitStack() as st0:
        es = st0.enter_context
        sd = es(nc.semaphore("sd"))
        block = es(nc.Block())
        emit(_mkctx(nc, block, es, dram, sd, 0, "s_"))
    return nc


def emit_ln_t(ctx):
    nc = ctx["nc"]; block = ctx["block"]; es = ctx["es"]
    PH = ctx["ph"]; SD = ctx["sd"]; PFX = ctx["prefix"]
    dram = ctx["dram"]
    x = dram["x"]
    idin = dram["idin"]
    xnT = dram["xnT"]
    with contextlib.ExitStack() as st:
        e = st.enter_context
        xs = [e(nc.sbuf_tensor(PFX + f"xs{i}", [128, D], F32)) for i in range(2)]
        sq = e(nc.sbuf_tensor(PFX + "sq", [128, D], F32))
        xnb = e(nc.sbuf_tensor(PFX + "xnb", [128, D], BF16))
        nsum = e(nc.sbuf_tensor(PFX + "nsum", [128, 1], F32))
        nm = e(nc.sbuf_tensor(PFX + "nm", [128, 1], F32))
        ssq = e(nc.sbuf_tensor(PFX + "ssq", [128, 1], F32))
        var = e(nc.sbuf_tensor(PFX + "var", [128, 1], F32))
        msq = e(nc.sbuf_tensor(PFX + "msq", [128, 1], F32))
        lnv = e(nc.sbuf_tensor(PFX + "lnv", [128, 1], F32))
        rstd = e(nc.sbuf_tensor(PFX + "rstd", [128, 1], F32))
        nmr = e(nc.sbuf_tensor(PFX + "nmr", [128, 1], F32))
        mhalf = e(nc.sbuf_tensor(PFX + "mhalf", [128, 1], F32))
        idsb = e(nc.sbuf_tensor(PFX + "idsb", [128, 128], F32))
        idb = e(nc.sbuf_tensor(PFX + "idb", [128, 128], BF16))
        xnT_sb = e(nc.sbuf_tensor(PFX + "xnT_sb", [128, KT * TOK], BF16))
        pt = [e(nc.psum_tensor(PFX + f"pt{i}", [128, 128], F32)) for i in range(2)]
        sld = es(nc.semaphore(PFX + "sld"))
        sv = es(nc.semaphore(PFX + "sv"))
        ss = es(nc.semaphore(PFX + "ss"))
        sp = es(nc.semaphore(PFX + "sp"))
        spf = es(nc.semaphore(PFX + "spf"))
        sst = es(nc.semaphore(PFX + "sst"))
        svs = es(nc.semaphore(PFX + "svs"))
        sss = es(nc.semaphore(PFX + "sss"))
        
        @block.sync
        def _(sync):
            if PH:
                sync.wait_ge(SD, PH)
            sync.dma_start(out=idsb[:, :], in_=idin).then_inc(sld, 16)
            for t in range(NT):
                if t >= 2:
                    sync.wait_ge(ss, t - 1)
                sync.dma_start(out=xs[t % 2][:, :],
                               in_=x[t * 128:(t + 1) * 128, :]).then_inc(sld, 16)
            sync.wait_ge(spf, NT * KT)
            sync.dma_start(
                out=xnT.rearrange("(k p) t -> p k t", p=128),
                in_=xnT_sb[:, :].rearrange("p (k t) -> p k t", k=KT),
            ).then_inc(sst, 16)
            sync.wait_ge(sst, 16)
            sync.nop().then_inc(SD, 1)

        @block.vector
        def _(vector):
            if PH:
                vector.wait_ge(SD, PH)
            ser = Ser(vector, svs)
            vector.wait_ge(sld, 16)
            vector.memset(mhalf[:, :], -0.5)
            vector.tensor_copy(idb[:, :], idsb[:, :]).then_inc(sv, 1)
            for t in range(NT):
                vector.wait_ge(sld, 32 + t * 16)
                xt = xs[t % 2]
                ser(vector.reduce_sum(nsum[:, :], xt[:, :], axis=AX.X, negate=True))
                ser(vector.tensor_scalar(nm[:, :], nsum[:, :], 1.0 / D, None, OP.mult))
                ser(vector.tensor_tensor(sq[:, :], xt[:, :], xt[:, :], OP.mult))
                ser(vector.reduce_sum(ssq[:, :], sq[:, :], axis=AX.X))
                ser(vector.tensor_scalar(var[:, :], ssq[:, :], 1.0 / D, None, OP.mult))
                ser(vector.tensor_tensor(msq[:, :], nm[:, :], nm[:, :], OP.mult))
                ser(vector.tensor_tensor(var[:, :], var[:, :], msq[:, :], OP.subtract))
                vector.tensor_scalar(var[:, :], var[:, :], EPS, None, OP.add).then_inc(sv, 1)
                for k in range(KT):
                    n = t * KT + k
                    vector.wait_ge(sp, n + 1)
                    vector.tensor_copy(
                        xnT_sb[:, (k * NT + t) * 128:(k * NT + t) * 128 + 128],
                        pt[n % 2][:, :]).then_inc(spf, 1)

        @block.scalar
        def _(scalar):
            if PH:
                scalar.wait_ge(SD, PH)
            sers = Ser(scalar, sss)
            for t in range(NT):
                scalar.wait_ge(sv, t + 2)
                sers(scalar.activation(lnv[:, :], var[:, :], AF.Ln))
                sers(scalar.activation(rstd[:, :], lnv[:, :], AF.Exp, scale=mhalf[:, 0:1]))
                sers(scalar.mul(nmr[:, :], nm[:, :], rstd[:, 0:1]))
                if t >= 1:
                    scalar.wait_ge(sp, t * KT)
                scalar.activation(xnb[:, :], xs[t % 2][:, :], AF.Identity,
                                  bias=nmr[:, 0:1], scale=rstd[:, 0:1]).then_inc(ss, 1)

        @block.tensor
        def _(tensor):
            if PH:
                tensor.wait_ge(SD, PH)
            for t in range(NT):
                tensor.wait_ge(ss, t + 1)
                for k in range(KT):
                    n = t * KT + k
                    if n >= 2:
                        tensor.wait_ge(spf, n - 1)
                    tensor.matmul(pt[n % 2][:, :], xnb[:, k * 128:(k + 1) * 128],
                                  idb[:, :], start=True, stop=True).then_inc(sp, 1)
    return nc



def emit_l0_attn(ctx):
    nc = ctx["nc"]; block = ctx["block"]; es = ctx["es"]
    PH = ctx["ph"]; SD = ctx["sd"]; PFX = ctx["prefix"]
    dram = ctx["dram"]
    xnT = dram["xnT"]
    w = dram["w"]
    mask = dram["mask"]
    bias = dram["bias"]
    x = dram["x"]
    et = dram["et"]
    idin = dram["idin"]
    y = dram["y"]
    bmn_d = dram["bmn_d"]

    NSUB = 4
    with contextlib.ExitStack() as st:
        e = st.enter_context
        msk = e(nc.sbuf_tensor(PFX + "msk", [128, D], I32))
        s1 = e(nc.sbuf_tensor(PFX + "s1", [128, 128], F32))
        a1 = [e(nc.sbuf_tensor(PFX + f"a1{i}", [128, 128], BF16)) for i in range(2)]
        etb = e(nc.sbuf_tensor(PFX + "etb", [128, 252], BF16))
        bmN = e(nc.sbuf_tensor(PFX + "bmN", [128, 128], BF16))
        bmT = e(nc.sbuf_tensor(PFX + "bmT", [128, 128], BF16))
        bmn = e(nc.sbuf_tensor(PFX + "bmn", [128, D], BF16))
        idsb = e(nc.sbuf_tensor(PFX + "idsb", [128, 128], F32))
        idb = e(nc.sbuf_tensor(PFX + "idb", [128, 128], BF16))
        xnT_sb = e(nc.sbuf_tensor(PFX + "xnT_sb", [128, KT * TOK], BF16))
        wrow = e(nc.sbuf_tensor(PFX + "wrow", [128, D], F32))
        wrb = [e(nc.sbuf_tensor(PFX + f"wrb{i}", [128, D], BF16)) for i in range(2)]
        wTm = e(nc.sbuf_tensor(PFX + "wTm", [128, KT * 512], BF16))
        bmx = e(nc.sbuf_tensor(PFX + "bmx", [128, KT * 512], BF16))
        biasb = e(nc.sbuf_tensor(PFX + "biasb", [1, D], BF16))
        onesr = e(nc.sbuf_tensor(PFX + "onesr", [1, 128], BF16))
        xsb = [e(nc.sbuf_tensor(PFX + f"xsb{i}", [128, 512], F32)) for i in range(2)]
        ysb = [e(nc.sbuf_tensor(PFX + f"ysb{i}", [128, 512], F32)) for i in range(2)]
        bm_ps = e(nc.psum_tensor(PFX + "bm_ps", [128, 128], F32))
        bt_ps = e(nc.psum_tensor(PFX + "bt_ps", [128, 128], F32))
        wt_ps = [e(nc.psum_tensor(PFX + f"wt_ps{i}", [128, 128], F32)) for i in range(2)]
        mm_ps = [e(nc.psum_tensor(PFX + f"mm_ps{i}", [128, 512], F32)) for i in range(2)]
        s_c = es(nc.semaphore(PFX + "s_c"))
        s_mk = es(nc.semaphore(PFX + "s_mk"))
        s_w = es(nc.semaphore(PFX + "s_w"))
        s_x = es(nc.semaphore(PFX + "s_x"))
        s_bx = es(nc.semaphore(PFX + "s_bx"))
        s_in = es(nc.semaphore(PFX + "s_in"))
        s_a1 = es(nc.semaphore(PFX + "s_a1"))
        s_pm = es(nc.semaphore(PFX + "s_pm"))
        s_bt = es(nc.semaphore(PFX + "s_bt"))
        s_vb = es(nc.semaphore(PFX + "s_vb"))
        s_tp = es(nc.semaphore(PFX + "s_tp"))
        s_wc2 = es(nc.semaphore(PFX + "s_wc2"))
        s_tt = es(nc.semaphore(PFX + "s_tt"))
        s_mm = es(nc.semaphore(PFX + "s_mm"))
        s_ev = es(nc.semaphore(PFX + "s_ev"))
        sst = es(nc.semaphore(PFX + "sst"))
        svs = es(nc.semaphore(PFX + "svs"))
        s_dbg = es(nc.semaphore(PFX + "s_dbg"))
        
        @block.sync
        def _(sync):
            if PH:
                sync.wait_ge(SD, PH)
            sync.dma_start(out=idsb[:, :], in_=idin).then_inc(s_c, 16)
            sync.dma_start(out=etb[:, :], in_=et).then_inc(s_c, 16)
            sync.dma_start(out=xnT_sb[:, :].rearrange("p (k t) -> p k t", k=KT),
                           in_=xnT.rearrange("(k p) t -> p k t", p=128)).then_inc(s_c, 16)
            for mt in range(KT):
                if mt >= 1:
                    sync.wait_ge(s_a1, mt)
                sync.dma_start(out=msk[:, :],
                               in_=mask[mt * 128:(mt + 1) * 128, :]).then_inc(s_mk, 16)
            sync.wait_ge(s_vb, 2)
            sync.dma_start(out=bmn_d.ap(), in_=bmn[:, :]).then_inc(s_bx, 16)
            sync.wait_ge(s_bx, 16)
            for nch in range(8):
                if nch >= 1:
                    sync.wait_ge(s_tt, nch * NSUB * KT)
                for g in range(4):
                    srcap = bass.AP(bmn_d, g * D + nch * 512,
                                    [[0, 32], [4 * D, KT], [1, 512]])
                    dst = bmx[g * 32:(g + 1) * 32, :].rearrange(
                        "p (k n) -> p k n", k=KT)
                    sync.dma_start(out=dst, in_=srcap).then_inc(s_bx, 16)
                for sub in range(NSUB):
                    i = nch * NSUB + sub
                    if i >= 1:
                        sync.wait_ge(s_wc2, i)
                    sync.dma_start(
                        out=wrow[:, :],
                        in_=w[nch * 512 + sub * 128:nch * 512 + sub * 128 + 128, :],
                    ).then_inc(s_w, 16)
                for tc in range(NT):
                    j = nch * NT + tc
                    if j >= 2:
                        sync.wait_ge(s_ev, j - 1)
                        sync.wait_ge(sst, (j - 1) * 16)
                    sync.dma_start(
                        out=xsb[j % 2][:, :],
                        in_=x[tc * 128:(tc + 1) * 128, nch * 512:(nch + 1) * 512],
                    ).then_inc(s_x, 16)
                    sync.wait_ge(s_ev, j + 1)
                    sync.dma_start(
                        out=y[tc * 128:(tc + 1) * 128, nch * 512:(nch + 1) * 512],
                        in_=ysb[j % 2][:, :]).then_inc(sst, 16)
            sync.wait_ge(sst, 64 * 16)
            sync.nop().then_inc(SD, 1)

        @block.gpsimd
        def _(gpsimd):
            if PH:
                gpsimd.wait_ge(SD, PH)
            gpsimd.dma_start(out=biasb[:, :], in_=bias).then_inc(s_c, 16)


        @block.vector
        def _(vector):
            if PH:
                vector.wait_ge(SD, PH)
            ser = Ser(vector, svs)
            vector.wait_ge(s_c, 64)
            vector.memset(onesr[:, :], 1.0)
            vector.tensor_copy(idb[:, :], idsb[:, :]).then_inc(s_in, 1)
            for mt in range(KT):
                vector.wait_ge(s_mk, (mt + 1) * 16)
                if mt >= 1:
                    vector.wait_ge(s_a1, mt)
                mr = msk[:, :].rearrange("p (a b) -> p a b", b=32)
                ser(vector.reduce_sum(s1[:, :], mr, axis=AX.X))
                if mt >= 2:
                    vector.wait_ge(s_pm, mt - 1)
                vector.tensor_scalar(a1[mt % 2][:, :], s1[:, :], 0, None,
                                     OP.is_gt).then_inc(s_a1, 1)
            vector.wait_ge(s_pm, KT)
            ser(vector.tensor_scalar(bmN[:, :], bm_ps[:, :], 0, None, OP.is_gt))
            vector.nop().then_inc(s_vb, 1)
            vector.wait_ge(s_bt, 1)
            ser(vector.tensor_copy(bmT[:, :], bt_ps[:, :]))
            for j in range(KT):
                ins = vector.tensor_copy(
                    bmn[:, :].rearrange("p (a b) -> p a b", b=32)[:, :, j:j + 1],
                    bmT[:, :].unsqueeze(2))
                if j == KT - 1:
                    ins.then_inc(s_vb, 1)
            for nch in range(8):
                vector.wait_ge(s_bx, 16 + (nch + 1) * 64)
                for sub in range(NSUB):
                    i = nch * NSUB + sub
                    vector.wait_ge(s_w, (i + 1) * 16)
                    if i >= 2:
                        vector.wait_ge(s_tp, (i - 1) * KT)
                    vector.tensor_copy(wrb[i % 2][:, :], wrow[:, :]).then_inc(s_wc2, 1)
                    for kt in range(KT):
                        n = i * KT + kt
                        off = kt * 512 + sub * 128
                        vector.wait_ge(s_tp, n + 1)
                        vector.tensor_tensor(
                            wTm[:, off:off + 128], wt_ps[n % 2][:, :],
                            bmx[:, off:off + 128], OP.mult).then_inc(s_tt, 1)
                for tc in range(NT):
                    j = nch * NT + tc
                    vector.wait_ge(s_mm, j + 1)
                    vector.wait_ge(s_x, (j + 1) * 16)
                    if j >= 2:
                        vector.wait_ge(sst, (j - 1) * 16)
                    vector.tensor_tensor(ysb[j % 2][:, :], mm_ps[j % 2][:, :],
                                         xsb[j % 2][:, :], OP.add).then_inc(s_ev, 1)

        @block.tensor
        def _(tensor):
            if PH:
                tensor.wait_ge(SD, PH)
            tensor.wait_ge(s_in, 1)
            for mt in range(KT):
                tensor.wait_ge(s_a1, mt + 1)
                tensor.matmul(bm_ps[:, :], etb[:, 124 - 4 * mt:252 - 4 * mt],
                              a1[mt % 2][:, :], start=(mt == 0),
                              stop=(mt == KT - 1)).then_inc(s_pm, 1)
            tensor.wait_ge(s_vb, 1)
            tensor.matmul(bt_ps[:, :], bmN[:, :], idb[:, :],
                          start=True, stop=True).then_inc(s_bt, 1)
            for nch in range(8):
                for sub in range(NSUB):
                    i = nch * NSUB + sub
                    tensor.wait_ge(s_wc2, i + 1)
                    for kt in range(KT):
                        n = i * KT + kt
                        if n >= 2:
                            tensor.wait_ge(s_tt, n - 1)
                        tensor.matmul(
                            wt_ps[n % 2][:, :],
                            wrb[i % 2][:, kt * 128:(kt + 1) * 128],
                            idb[:, :], start=True, stop=True).then_inc(s_tp, 1)
                tensor.wait_ge(s_tt, (nch * NSUB + NSUB) * KT)
                for tc in range(NT):
                    j = nch * NT + tc
                    if j >= 2:
                        tensor.wait_ge(s_ev, j - 1)
                    for kt in range(KT):
                        tensor.matmul(
                            mm_ps[j % 2][:, :],
                            xnT_sb[:, kt * TOK + tc * 128:kt * TOK + tc * 128 + 128],
                            wTm[:, kt * 512:(kt + 1) * 512],
                            start=(kt == 0), stop=False)
                    tensor.matmul(mm_ps[j % 2][:, :], onesr[0:1, :],
                                  biasb[0:1, nch * 512:(nch + 1) * 512],
                                  start=False, stop=True).then_inc(s_mm, 1)
    return nc



def emit_l0_ffn(ctx):
    nc = ctx["nc"]; block = ctx["block"]; es = ctx["es"]
    PH = ctx["ph"]; SD = ctx["sd"]; PFX = ctx["prefix"]
    dram = ctx["dram"]
    xnT = dram["xnT"]
    qwg = dram["qwg"]
    zg = dram["zg"]
    sg = dram["sg"]
    qwu = dram["qwu"]
    zu = dram["zu"]
    su = dram["su"]
    qwd = dram["qwd"]
    zd = dram["zd"]
    sd = dram["sd_"]
    yy = dram["yy"]
    icb = dram["icb"]
    idbnd = dram["idbnd"]
    x1 = dram["x1"]
    sgb_d = dram["sgb_d"]
    sub_d = dram["sub_d"]
    sdb_d = dram["sdb_d"]
    nzg_d = dram["nzg_d"]
    nzu_d = dram["nzu_d"]
    nzd_d = dram["nzd_d"]
    hgt_d = dram["hgt_d"]

    PW = 1376
    DCH = 8
    DNCH = (NHT + DCH - 1) // DCH
    PRE = ((32, HFF, zg, sg, sgb_d, nzg_d),
           (32, HFF, zu, su, sub_d, nzu_d),
           (86, D, zd, sd, sdb_d, nzd_d))
    NPRE = sum((w + PW - 1) // PW for _, w, *_ in PRE)

    with contextlib.ExitStack() as st:
        e = st.enter_context
        zpre = e(nc.sbuf_tensor(PFX + "zpre", [86, PW], I32))
        spre = e(nc.sbuf_tensor(PFX + "spre", [86, PW], F32))
        nzf = e(nc.sbuf_tensor(PFX + "nzf", [86, PW], F32))
        nzb = e(nc.sbuf_tensor(PFX + "nzb", [86, PW], BF16))
        xnh = e(nc.sbuf_tensor(PFX + "xnh", [128, KT * 512], BF16))
        qbuf = e(nc.sbuf_tensor(PFX + "qbuf", [128, KT * 128], I32))
        sexp = e(nc.sbuf_tensor(PFX + "sexp", [128, KT * 128], BF16))
        wb = e(nc.sbuf_tensor(PFX + "wb", [128, KT * 128], BF16))
        qdb = e(nc.sbuf_tensor(PFX + "qdb", [128, DCH * 512], I32))
        sexd = e(nc.sbuf_tensor(PFX + "sexd", [128, DCH * 512], BF16))
        wdb = e(nc.sbuf_tensor(PFX + "wdb", [128, DCH * 512], BF16))
        hgch = [e(nc.sbuf_tensor(PFX + f"hgch{i}", [128, DCH * 512], BF16)) for i in range(2)]
        hgs = [e(nc.sbuf_tensor(PFX + f"hgs{i}", [128, 512], BF16)) for i in range(2)]
        nz8g = e(nc.sbuf_tensor(PFX + "nz8g", [32, 8 * 128], BF16))
        nz8u = e(nc.sbuf_tensor(PFX + "nz8u", [32, 8 * 128], BF16))
        nzdc = e(nc.sbuf_tensor(PFX + "nzdc", [86, 512], BF16))
        csum = e(nc.sbuf_tensor(PFX + "csum", [32, 512], BF16))
        chs = e(nc.sbuf_tensor(PFX + "chs", [86, 512], BF16))
        icbs = e(nc.sbuf_tensor(PFX + "icbs", [128, 63], BF16))
        idbs = e(nc.sbuf_tensor(PFX + "idbs", [128, 171], BF16))
        sgl = e(nc.sbuf_tensor(PFX + "sgl", [128, 512], F32))
        ysl = [e(nc.sbuf_tensor(PFX + f"ysl{i}", [128, 512], F32)) for i in range(2)]
        osl = [e(nc.sbuf_tensor(PFX + f"osl{i}", [128, 512], F32)) for i in range(2)]
        g_ps = e(nc.psum_tensor(PFX + "g_ps", [128, 512], F32))
        u_ps = e(nc.psum_tensor(PFX + "u_ps", [128, 512], F32))
        c_ps = e(nc.psum_tensor(PFX + "c_ps", [128, 512], F32))
        o_ps = [e(nc.psum_tensor(PFX + f"o_ps{i}", [128, 512], F32)) for i in range(4)]
        s_pre = es(nc.semaphore(PFX + "s_pre"))
        s_prev = es(nc.semaphore(PFX + "s_prev"))
        s_preo = es(nc.semaphore(PFX + "s_preo"))
        s_cb = es(nc.semaphore(PFX + "s_cb"))
        s_xh = es(nc.semaphore(PFX + "s_xh"))
        s_nzg = es(nc.semaphore(PFX + "s_nzg"))
        s_q = es(nc.semaphore(PFX + "s_q"))
        s_dq = es(nc.semaphore(PFX + "s_dq"))
        s_gu = es(nc.semaphore(PFX + "s_gu"))
        s_sl = es(nc.semaphore(PFX + "s_sl"))
        s_hg = es(nc.semaphore(PFX + "s_hg"))
        s_hgs = es(nc.semaphore(PFX + "s_hgs"))
        s_chm = es(nc.semaphore(PFX + "s_chm"))
        s_cs = es(nc.semaphore(PFX + "s_cs"))
        s_cv = es(nc.semaphore(PFX + "s_cv"))
        s_qd = es(nc.semaphore(PFX + "s_qd"))
        s_hgc = es(nc.semaphore(PFX + "s_hgc"))
        s_dqd = es(nc.semaphore(PFX + "s_dqd"))
        s_wch = es(nc.semaphore(PFX + "s_wch"))
        s_nzd = es(nc.semaphore(PFX + "s_nzd"))
        s_om = es(nc.semaphore(PFX + "s_om"))
        s_y = es(nc.semaphore(PFX + "s_y"))
        s_ev = es(nc.semaphore(PFX + "s_ev"))
        sst = es(nc.semaphore(PFX + "sst"))
        svs = es(nc.semaphore(PFX + "svs"))
        
        @block.gpsimd
        def _(gp):
            if PH:
                gp.wait_ge(SD, PH)
            n = 0
            for rows, width, zt, st_, sb_d, nz_d in PRE:
                for c in range((width + PW - 1) // PW):
                    w0 = c * PW
                    wd = min(width, w0 + PW) - w0
                    if n >= 1:
                        gp.wait_ge(s_preo, n * 32)
                    gp.dma_start(out=zpre[0:rows, 0:wd], in_=zt[:, w0:w0 + wd]).then_inc(s_pre, 16)
                    gp.dma_start(out=spre[0:rows, 0:wd], in_=st_[:, w0:w0 + wd]).then_inc(s_pre, 16)
                    gp.wait_ge(s_prev, n + 1)
                    gp.dma_start(out=sb_d.ap()[:, w0:w0 + wd], in_=spre[0:rows, 0:wd]).then_inc(s_preo, 16)
                    gp.dma_start(out=nz_d.ap()[:, w0:w0 + wd], in_=nzb[0:rows, 0:wd]).then_inc(s_preo, 16)
                    n += 1
            gp.wait_ge(s_preo, NPRE * 32)
            gp.nop().then_inc(s_preo, 1)
            for half in range(2):
                for ht in range(NHT):
                    hh = half * NHT + ht
                    gp.wait_ge(s_hg, hh + 1)
                    gp.dma_start(out=hgt_d.ap()[:, ht * 512:(ht + 1) * 512],
                                 in_=hgs[hh % 2][:, :]).then_inc(s_hgs, 16)
                for dc in range(8):
                    for tc in range(4):
                        j = (half * 8 + dc) * 4 + tc
                        r0 = half * 512 + tc * 128
                        if j >= 2:
                            gp.wait_ge(s_ev, j - 1)
                        gp.dma_start(out=ysl[j % 2][:, :],
                                     in_=yy[r0:r0 + 128, dc * 512:(dc + 1) * 512]).then_inc(s_y, 16)
                        gp.wait_ge(s_ev, j + 1)
                        gp.dma_start(out=x1[r0:r0 + 128, dc * 512:(dc + 1) * 512],
                                     in_=osl[j % 2][:, :]).then_inc(sst, 16)
            gp.wait_ge(sst, 64 * 16)
            gp.nop().then_inc(SD, 1)

        @block.scalar
        def _(sc):
            if PH:
                sc.wait_ge(SD, PH)
            sc.dma_start(out=icbs[:, :], in_=icb).then_inc(s_cb, 16)
            sc.dma_start(out=idbs[:, :], in_=idbnd).then_inc(s_cb, 16)
            sc.wait_ge(s_preo, NPRE * 32 + 1)
            for half in range(2):
                for ht in range(NHT):
                    hh = half * NHT + ht
                    if ht % 8 == 0:
                        if hh > 0:
                            sc.wait_ge(s_gu, hh * 2)
                        w0 = ht * 128
                        wd = min(HFF, w0 + 8 * 128) - w0
                        sc.dma_start(out=nz8g[:, 0:wd], in_=nzg_d.ap()[:, w0:w0 + wd]).then_inc(s_nzg, 16)
                        sc.dma_start(out=nz8u[:, 0:wd], in_=nzu_d.ap()[:, w0:w0 + wd]).then_inc(s_nzg, 16)
                    for m, s_d in enumerate((sgb_d, sub_d)):
                        k = hh * 2 + m
                        if k >= 1:
                            sc.wait_ge(s_dq, k)
                        srcap = bass.AP(s_d, ht * 128, [[0, 128], [HFF, KT], [1, 128]])
                        sc.dma_start(out=sexp[:, :].rearrange("p (g h) -> p g h", g=KT),
                                     in_=srcap).then_inc(s_q, 16)
                    sc.wait_ge(s_gu, hh * 2 + 1)
                    sc.wait_ge(s_hg, hh)
                    sc.activation(sgl[:, :], g_ps[:, :], AF.Silu).then_inc(s_sl, 1)
                for dc in range(8):
                    dj = half * 8 + dc
                    if dj >= 1:
                        sc.wait_ge(s_om, dj * 4)
                    sc.dma_start(out=nzdc[:, :],
                                 in_=nzd_d.ap()[:, dc * 512:(dc + 1) * 512]).then_inc(s_nzd, 16)
                    for c in range(DNCH):
                        kk = dj * DNCH + c
                        nh = min(DCH, NHT - c * DCH)
                        if kk >= 1:
                            sc.wait_ge(s_dqd, kk)
                        srcap = bass.AP(sdb_d, (c * DCH) * D + dc * 512,
                                        [[0, 128], [D, nh], [1, 512]])
                        sc.dma_start(out=sexd[:, 0:nh * 512].rearrange("p (a b) -> p a b", a=nh),
                                     in_=srcap).then_inc(s_qd, 16)

        @block.sync
        def _(sync):
            if PH:
                sync.wait_ge(SD, PH)
            for half in range(2):
                if half == 1:
                    sync.wait_ge(s_gu, NHT * 2)
                srcap = bass.AP(xnT.tensor, half * 512,
                                [[TOK, 128], [128 * TOK, KT], [1, 512]])
                sync.dma_start(out=xnh[:, :].rearrange("p (g t) -> p g t", g=KT),
                               in_=srcap).then_inc(s_xh, 16)
                for ht in range(NHT):
                    for m, qw_ in enumerate((qwg, qwu)):
                        k = (half * NHT + ht) * 2 + m
                        if k >= 1:
                            sync.wait_ge(s_dq, k)
                        srcap = bass.AP(qw_.tensor, ht * 128,
                                        [[HFF, 128], [128 * HFF, KT], [1, 128]])
                        sync.dma_start(out=qbuf[:, :].rearrange("p (g h) -> p g h", g=KT),
                                       in_=srcap).then_inc(s_q, 16)
                for dc in range(8):
                    dj = half * 8 + dc
                    for c in range(DNCH):
                        kk = dj * DNCH + c
                        nh = min(DCH, NHT - c * DCH)
                        if kk >= 1:
                            sync.wait_ge(s_dqd, kk)
                        srcap = bass.AP(qwd.tensor, (c * DCH * 128) * D + dc * 512,
                                        [[D, 128], [128 * D, nh], [1, 512]])
                        sync.dma_start(
                            out=qdb[:, 0:nh * 512].rearrange("p (a b) -> p a b", a=nh),
                            in_=srcap).then_inc(s_qd, 16)
                        if kk >= 2:
                            sync.wait_ge(s_wch, kk - 1)
                        sync.dma_start(
                            out=hgch[kk % 2][:, 0:nh * 512],
                            in_=hgt_d.ap()[:, c * DCH * 512:(c * DCH + nh) * 512],
                        ).then_inc(s_hgc, 16)

        @block.vector
        def _(vec):
            if PH:
                vec.wait_ge(SD, PH)
            ser = Ser(vec, svs)
            n = 0
            for rows, width, *_ in PRE:
                for c in range((width + PW - 1) // PW):
                    wd = min(width, (c + 1) * PW) - c * PW
                    vec.wait_ge(s_pre, (n + 1) * 32)
                    if n >= 1:
                        vec.wait_ge(s_preo, n * 32)
                    ser(vec.tensor_tensor(nzf[0:rows, 0:wd], zpre[0:rows, 0:wd],
                                          spre[0:rows, 0:wd], OP.mult))
                    vec.tensor_scalar(nzb[0:rows, 0:wd], nzf[0:rows, 0:wd], -1.0,
                                      None, OP.mult).then_inc(s_prev, 1)
                    n += 1
            for half in range(2):
                vec.wait_ge(s_cs, half + 1)
                vec.tensor_copy(csum[:, :], c_ps[0:32, :]).then_inc(s_cv, 1)
                for ht in range(NHT):
                    hh = half * NHT + ht
                    for m in range(2):
                        k = hh * 2 + m
                        vec.wait_ge(s_q, (k + 1) * 32)
                        if k >= 1:
                            vec.wait_ge(s_gu, k)
                        vec.tensor_tensor(wb[:, :], qbuf[:, :], sexp[:, :],
                                          OP.mult).then_inc(s_dq, 1)
                    vec.wait_ge(s_sl, hh + 1)
                    vec.wait_ge(s_gu, hh * 2 + 2)
                    if hh >= 2:
                        vec.wait_ge(s_hgs, (hh - 1) * 16)
                        vec.wait_ge(s_chm, hh - 1)
                    vec.tensor_tensor(hgs[hh % 2][:, :], sgl[:, :],
                                      u_ps[:, :], OP.mult).then_inc(s_hg, 1)
                vec.wait_ge(s_chm, (half + 1) * NHT)
                vec.tensor_copy(chs[:, :], c_ps[0:86, :]).then_inc(s_cv, 1)
                for dc in range(8):
                    dj = half * 8 + dc
                    for c in range(DNCH):
                        kk = dj * DNCH + c
                        nh = min(DCH, NHT - c * DCH)
                        vec.wait_ge(s_qd, (kk + 1) * 32)
                        if kk >= 1:
                            vec.wait_ge(s_wch, kk)
                        vec.tensor_tensor(wdb[:, 0:nh * 512], qdb[:, 0:nh * 512],
                                          sexd[:, 0:nh * 512], OP.mult).then_inc(s_dqd, 1)
                    for tc in range(4):
                        j = dj * 4 + tc
                        vec.wait_ge(s_om, j + 1)
                        vec.wait_ge(s_y, (j + 1) * 16)
                        if j >= 2:
                            vec.wait_ge(sst, (j - 1) * 16)
                        vec.tensor_tensor(osl[j % 2][:, :], o_ps[tc][:, :],
                                          ysl[j % 2][:, :], OP.add).then_inc(s_ev, 1)

        @block.tensor
        def _(te):
            if PH:
                te.wait_ge(SD, PH)
            te.wait_ge(s_cb, 32)
            for half in range(2):
                te.wait_ge(s_cv, half * 2)
                te.wait_ge(s_xh, (half + 1) * 16)
                for g in range(KT):
                    ins = te.matmul(c_ps[0:32, :], icbs[:, 31 - g:63 - g],
                                    xnh[:, g * 512:(g + 1) * 512],
                                    start=(g == 0), stop=(g == KT - 1))
                ins.then_inc(s_cs, 1)
                te.wait_ge(s_cv, half * 2 + 1)
                for ht in range(NHT):
                    hh = half * NHT + ht
                    if ht % 8 == 0:
                        nb = half * ((NHT + 7) // 8) + ht // 8
                        te.wait_ge(s_nzg, (nb + 1) * 32)
                    for m in range(2):
                        k = hh * 2 + m
                        te.wait_ge(s_dq, k + 1)
                        ps = g_ps if m == 0 else u_ps
                        if m == 0:
                            te.wait_ge(s_sl, hh)
                        else:
                            te.wait_ge(s_hg, hh)
                        nzt = nz8g if m == 0 else nz8u
                        for g in range(KT):
                            te.matmul(ps[:, :], wb[:, g * 128:(g + 1) * 128],
                                      xnh[:, g * 512:(g + 1) * 512],
                                      start=(g == 0), stop=False)
                        te.matmul(ps[:, :], nzt[:, (ht % 8) * 128:(ht % 8) * 128 + 128],
                                  csum[0:32, :], start=False, stop=True).then_inc(s_gu, 1)
                    if ht >= 1:
                        phh = hh - 1
                        te.wait_ge(s_hg, phh + 1)
                        te.matmul(c_ps[0:86, :], idbs[:, 85 - (ht - 1):171 - (ht - 1)],
                                  hgs[phh % 2][:, :], start=(ht - 1 == 0),
                                  stop=False).then_inc(s_chm, 1)
                te.wait_ge(s_hg, (half + 1) * NHT)
                te.matmul(c_ps[0:86, :], idbs[:, 85 - (NHT - 1):171 - (NHT - 1)],
                          hgs[(half * NHT + NHT - 1) % 2][:, :], start=False,
                          stop=True).then_inc(s_chm, 1)
                te.wait_ge(s_cv, half * 2 + 2)
                for dc in range(8):
                    dj = half * 8 + dc
                    te.wait_ge(s_ev, dj * 4)
                    te.wait_ge(s_nzd, (dj + 1) * 16)
                    for c in range(DNCH):
                        kk = dj * DNCH + c
                        nh = min(DCH, NHT - c * DCH)
                        te.wait_ge(s_dqd, kk + 1)
                        te.wait_ge(s_hgc, (kk + 1) * 16)
                        for hl in range(nh):
                            hs = c * DCH + hl
                            for tc in range(4):
                                ins = te.matmul(
                                    o_ps[tc][:, :],
                                    hgch[kk % 2][:, hl * 512 + tc * 128:hl * 512 + tc * 128 + 128],
                                    wdb[:, hl * 512:(hl + 1) * 512],
                                    start=(hs == 0), stop=False)
                        ins.then_inc(s_wch, 1)
                    for tc in range(4):
                        te.matmul(o_ps[tc][:, :], chs[:, tc * 128:(tc + 1) * 128],
                                  nzdc[:, :], start=False, stop=True).then_inc(s_om, 1)
    return nc



def emit_qkv(ctx):
    nc = ctx["nc"]; block = ctx["block"]; es = ctx["es"]
    PH = ctx["ph"]; SD = ctx["sd"]; PFX = ctx["prefix"]
    dram = ctx["dram"]
    x = dram["x"]
    wq = dram["wq"]
    bq = dram["bq"]
    wk = dram["wk"]
    bk = dram["bk"]
    wv = dram["wv"]
    bv = dram["bv"]
    cq = dram["cq"]
    sq_ = dram["sq_"]
    ck = dram["ck"]
    sk_ = dram["sk_"]
    idin = dram["idin"]
    xn1T = dram["xn1T"]
    qT = dram["qT"]
    ksh = dram["ksh"]
    vsh = dram["vsh"]

    NQ, NK = KT, 8
    JOBS = ([(0, ft, qc) for ft in range(NQ) for qc in range(2)]
            + [(1, ft, qc) for ft in range(NK) for qc in range(2)]
            + [(2, ft, qc) for ft in range(NK) for qc in range(2)])

    with contextlib.ExitStack() as st:
        e = st.enter_context
        xs = [e(nc.sbuf_tensor(PFX + f"xs{i}", [128, D], F32)) for i in range(2)]
        sqb = e(nc.sbuf_tensor(PFX + "sqb", [128, D], F32))
        xnb = e(nc.sbuf_tensor(PFX + "xnb", [128, D], BF16))
        nsum = e(nc.sbuf_tensor(PFX + "nsum", [128, 1], F32))
        nm = e(nc.sbuf_tensor(PFX + "nm", [128, 1], F32))
        ssq = e(nc.sbuf_tensor(PFX + "ssq", [128, 1], F32))
        var = e(nc.sbuf_tensor(PFX + "var", [128, 1], F32))
        msq = e(nc.sbuf_tensor(PFX + "msq", [128, 1], F32))
        lnv = e(nc.sbuf_tensor(PFX + "lnv", [128, 1], F32))
        rstd = e(nc.sbuf_tensor(PFX + "rstd", [128, 1], F32))
        nmr = e(nc.sbuf_tensor(PFX + "nmr", [128, 1], F32))
        mhalf = e(nc.sbuf_tensor(PFX + "mhalf", [128, 1], F32))
        idsb = e(nc.sbuf_tensor(PFX + "idsb", [128, 128], F32))
        idb = e(nc.sbuf_tensor(PFX + "idb", [128, 128], BF16))
        xnT_sb = e(nc.sbuf_tensor(PFX + "xnT_sb", [128, KT * TOK], BF16))
        wf = [e(nc.sbuf_tensor(PFX + f"wf{i}", [128, KT * 128], F32)) for i in range(2)]
        wbf = [e(nc.sbuf_tensor(PFX + f"wbf{i}", [128, KT * 128], BF16)) for i in range(2)]
        bqb = e(nc.sbuf_tensor(PFX + "bqb", [1, D], BF16))
        bkb = e(nc.sbuf_tensor(PFX + "bkb", [1, 1024], BF16))
        bvb = e(nc.sbuf_tensor(PFX + "bvb", [1, 1024], BF16))
        onesr = e(nc.sbuf_tensor(PFX + "onesr", [1, 512], BF16))
        cqs = e(nc.sbuf_tensor(PFX + "cqs", [64, TOK], F32))
        sqs = e(nc.sbuf_tensor(PFX + "sqs", [64, TOK], F32))
        cks = e(nc.sbuf_tensor(PFX + "cks", [64, TOK], F32))
        sks = e(nc.sbuf_tensor(PFX + "sks", [64, TOK], F32))
        tm1 = e(nc.sbuf_tensor(PFX + "tm1", [64, 512], F32))
        tm2 = e(nc.sbuf_tensor(PFX + "tm2", [64, 512], F32))
        qb = [e(nc.sbuf_tensor(PFX + f"qb{i}", [128, 512], BF16)) for i in range(2)]
        vtb = e(nc.sbuf_tensor(PFX + "vtb", [128, 512], BF16))
        pt = [e(nc.psum_tensor(PFX + f"pt{i}", [128, 128], F32)) for i in range(2)]
        q_ps = [e(nc.psum_tensor(PFX + f"q_ps{i}", [128, 512], F32)) for i in range(2)]
        pt2 = [e(nc.psum_tensor(PFX + f"pt2{i}", [128, 128], F32)) for i in range(2)]
        s_c = es(nc.semaphore(PFX + "s_c"))
        s_x = es(nc.semaphore(PFX + "s_x"))
        sv = es(nc.semaphore(PFX + "sv"))
        ss = es(nc.semaphore(PFX + "ss"))
        sp = es(nc.semaphore(PFX + "sp"))
        spf = es(nc.semaphore(PFX + "spf"))
        s_w = es(nc.semaphore(PFX + "s_w"))
        s_wc = es(nc.semaphore(PFX + "s_wc"))
        s_mm = es(nc.semaphore(PFX + "s_mm"))
        s_rp = es(nc.semaphore(PFX + "s_rp"))
        s_vt = es(nc.semaphore(PFX + "s_vt"))
        s_v2 = es(nc.semaphore(PFX + "s_v2"))
        s_v3 = es(nc.semaphore(PFX + "s_v3"))
        sst = es(nc.semaphore(PFX + "sst"))
        s_xo = es(nc.semaphore(PFX + "s_xo"))
        svs = es(nc.semaphore(PFX + "svs"))
        sss = es(nc.semaphore(PFX + "sss"))
        
        @block.sync
        def _(sync):
            if PH:
                sync.wait_ge(SD, PH)
            sync.dma_start(out=idsb[:, :], in_=idin).then_inc(s_c, 16)
            sync.dma_start(out=cqs[:, :], in_=cq).then_inc(s_c, 16)
            sync.dma_start(out=sqs[:, :], in_=sq_).then_inc(s_c, 16)
            sync.dma_start(out=cks[:, :], in_=ck).then_inc(s_c, 16)
            sync.dma_start(out=sks[:, :], in_=sk_).then_inc(s_c, 16)
            for t in range(NT):
                if t >= 2:
                    sync.wait_ge(ss, t - 1)
                sync.dma_start(out=xs[t % 2][:, :],
                               in_=x[t * 128:(t + 1) * 128, :]).then_inc(s_x, 16)
            sync.wait_ge(spf, NT * KT)
            sync.dma_start(
                out=xn1T.rearrange("(k p) t -> p k t", p=128),
                in_=xnT_sb[:, :].rearrange("p (k t) -> p k t", k=KT),
            ).then_inc(s_xo, 16)
            nw = 0
            for wsrc, nf, ncol in ((wq, NQ, D), (wk, NK, 1024), (wv, NK, 1024)):
                for ft in range(nf):
                    if nw >= 2:
                        sync.wait_ge(s_wc, nw - 1)
                    srcap = bass.AP(wsrc.tensor, ft * 128,
                                    [[ncol, 128], [128 * ncol, KT], [1, 128]])
                    sync.dma_start(out=wf[nw % 2][:, :].rearrange("p (a b) -> p a b", a=KT),
                                   in_=srcap).then_inc(s_w, 16)
                    nw += 1
            sync.wait_ge(sst, len(JOBS) * 16)
            sync.wait_ge(s_xo, 16)
            sync.nop().then_inc(SD, 1)

        @block.gpsimd
        def _(gp):
            if PH:
                gp.wait_ge(SD, PH)
            gp.dma_start(out=bqb[:, :], in_=bq).then_inc(s_c, 16)
            gp.dma_start(out=bkb[:, :], in_=bk).then_inc(s_c, 16)
            gp.dma_start(out=bvb[:, :], in_=bv).then_inc(s_c, 16)
            for i, (kind, ft, qc) in enumerate(JOBS):
                gp.wait_ge(s_rp, i + 1)
                if kind == 0:
                    gp.dma_start(out=qT[ft * 128:(ft + 1) * 128, qc * 512:(qc + 1) * 512],
                                 in_=qb[i % 2][:, :]).then_inc(sst, 16)
                elif kind == 1:
                    gp.dma_start(out=ksh[ft * 128:(ft + 1) * 128, qc * 512:(qc + 1) * 512],
                                 in_=qb[i % 2][:, :]).then_inc(sst, 16)
                else:
                    dstap = bass.AP(vsh.tensor, (qc * 4) * 128 * 1024 + ft * 128,
                                    [[1024, 128], [128 * 1024, 4], [1, 128]])
                    gp.dma_start(out=dstap,
                                 in_=qb[i % 2][:, :].rearrange("p (a b) -> p a b", a=4)).then_inc(sst, 16)

        @block.vector
        def _(vec):
            if PH:
                vec.wait_ge(SD, PH)
            ser = Ser(vec, svs)
            vec.wait_ge(s_c, 16)
            vec.memset(mhalf[:, :], -0.5)
            vec.memset(onesr[:, :], 1.0)
            vec.tensor_copy(idb[:, :], idsb[:, :]).then_inc(sv, 1)
            for t in range(NT):
                vec.wait_ge(s_x, (t + 1) * 16)
                xt = xs[t % 2]
                ser(vec.reduce_sum(nsum[:, :], xt[:, :], axis=AX.X, negate=True))
                ser(vec.tensor_scalar(nm[:, :], nsum[:, :], 1.0 / D, None, OP.mult))
                ser(vec.tensor_tensor(sqb[:, :], xt[:, :], xt[:, :], OP.mult))
                ser(vec.reduce_sum(ssq[:, :], sqb[:, :], axis=AX.X))
                ser(vec.tensor_scalar(var[:, :], ssq[:, :], 1.0 / D, None, OP.mult))
                ser(vec.tensor_tensor(msq[:, :], nm[:, :], nm[:, :], OP.mult))
                ser(vec.tensor_tensor(var[:, :], var[:, :], msq[:, :], OP.subtract))
                vec.tensor_scalar(var[:, :], var[:, :], EPS, None, OP.add).then_inc(sv, 1)
                for k in range(KT):
                    n = t * KT + k
                    vec.wait_ge(sp, n + 1)
                    vec.tensor_copy(
                        xnT_sb[:, (k * NT + t) * 128:(k * NT + t) * 128 + 128],
                        pt[n % 2][:, :]).then_inc(spf, 1)
            NW = NQ + 2 * NK

            def cast(nw):
                vec.wait_ge(s_w, (nw + 1) * 16)
                if nw >= 2:
                    vec.wait_ge(s_mm, (nw - 1) * 2)
                vec.tensor_copy(wbf[nw % 2][:, :], wf[nw % 2][:, :]).then_inc(s_wc, 1)

            cast(0)
            nvt = 0
            for i, (kind, ft, qc) in enumerate(JOBS):
                if i % 2 == 0 and i // 2 + 1 < NW:
                    cast(i // 2 + 1)
                vec.wait_ge(s_mm, i + 1)
                ps = q_ps[i % 2]
                if i >= 2:
                    vec.wait_ge(sst, (i - 1) * 16)
                ob = qb[i % 2]
                if kind < 2:
                    cst, snt = (cqs, sqs) if kind == 0 else (cks, sks)
                    cs_ = cst[:, qc * 512:(qc + 1) * 512]
                    sn_ = snt[:, qc * 512:(qc + 1) * 512]
                    ser(vec.tensor_tensor(tm1[:, :], ps[0:64, :], cs_, OP.mult))
                    ser(vec.tensor_tensor(tm2[:, :], ps[64:128, :], sn_, OP.mult))
                    ser(vec.tensor_tensor(ob[0:64, :], tm1[:, :], tm2[:, :], OP.subtract))
                    ser(vec.tensor_tensor(tm1[:, :], ps[0:64, :], sn_, OP.mult))
                    ser(vec.tensor_tensor(tm2[:, :], ps[64:128, :], cs_, OP.mult))
                    vec.tensor_tensor(ob[64:128, :], tm1[:, :], tm2[:, :],
                                      OP.add).then_inc(s_rp, 1)
                else:
                    if nvt >= 1:
                        vec.wait_ge(s_v2, nvt * 4)
                    vec.tensor_copy(vtb[:, :], ps[:, :]).then_inc(s_vt, 1)
                    for j in range(4):
                        vec.wait_ge(s_v2, nvt * 4 + j + 1)
                        vec.tensor_copy(ob[:, j * 128:(j + 1) * 128],
                                        pt2[j % 2][:, :]).then_inc(s_v3, 1)
                    vec.nop().then_inc(s_rp, 1)
                    nvt += 1

        @block.scalar
        def _(sc):
            if PH:
                sc.wait_ge(SD, PH)
            sers = Ser(sc, sss)
            for t in range(NT):
                sc.wait_ge(sv, t + 2)
                sers(sc.activation(lnv[:, :], var[:, :], AF.Ln))
                sers(sc.activation(rstd[:, :], lnv[:, :], AF.Exp, scale=mhalf[:, 0:1]))
                sers(sc.mul(nmr[:, :], nm[:, :], rstd[:, 0:1]))
                if t >= 1:
                    sc.wait_ge(sp, t * KT)
                sc.activation(xnb[:, :], xs[t % 2][:, :], AF.Identity,
                              bias=nmr[:, 0:1], scale=rstd[:, 0:1]).then_inc(ss, 1)

        @block.tensor
        def _(te):
            if PH:
                te.wait_ge(SD, PH)
            for t in range(NT):
                te.wait_ge(ss, t + 1)
                for k in range(KT):
                    n = t * KT + k
                    if n >= 2:
                        te.wait_ge(spf, n - 1)
                    te.matmul(pt[n % 2][:, :], xnb[:, k * 128:(k + 1) * 128],
                              idb[:, :], start=True, stop=True).then_inc(sp, 1)
            nvt = 0
            for i, (kind, ft, qc) in enumerate(JOBS):
                nw = i // 2
                te.wait_ge(s_wc, nw + 1)
                ps = q_ps[i % 2]
                if i >= 2:
                    te.wait_ge(s_rp, i - 1)
                bias_sb = (bqb, bkb, bvb)[kind]
                for dt in range(KT):
                    te.matmul(ps[:, :], wbf[nw % 2][:, dt * 128:(dt + 1) * 128],
                              xnT_sb[:, dt * TOK + qc * 512:dt * TOK + qc * 512 + 512],
                              start=(dt == 0), stop=False)
                te.matmul(ps[:, :], bias_sb[0:1, ft * 128:(ft + 1) * 128],
                          onesr[0:1, :], start=False, stop=True).then_inc(s_mm, 1)
                if kind == 2:
                    te.wait_ge(s_vt, nvt + 1)
                    for j in range(4):
                        if nvt * 4 + j >= 2:
                            te.wait_ge(s_v3, nvt * 4 + j - 1)
                        te.matmul(pt2[j % 2][:, :], vtb[:, j * 128:(j + 1) * 128],
                                  idb[:, :], start=True, stop=True).then_inc(s_v2, 1)
                    nvt += 1
    return nc



def emit_attn(ctx):
    nc = ctx["nc"]; block = ctx["block"]; es = ctx["es"]
    PH = ctx["ph"]; SD = ctx["sd"]; PFX = ctx["prefix"]
    dram = ctx["dram"]
    qTd = dram["qTd"]
    ktf = dram["ktf"]
    vtf = dram["vtf"]
    idin = dram["idin"]
    attn = dram["attn"]

    SKT = S // 128
    with contextlib.ExitStack() as st:
        e = st.enter_context
        ktb = e(nc.sbuf_tensor(PFX + "ktb", [128, 8 * S], BF16))
        vtb = e(nc.sbuf_tensor(PFX + "vtb", [128, SKT * 1024], BF16))
        prb = e(nc.sbuf_tensor(PFX + "prb", [128, SKT * 512], BF16))
        qth = [e(nc.sbuf_tensor(PFX + f"qth{i}", [128, TOK], BF16)) for i in range(2)]
        avb = e(nc.sbuf_tensor(PFX + "avb", [128, 512], BF16))
        avn = e(nc.sbuf_tensor(PFX + "avn", [128, 512], F32))
        dnf = e(nc.sbuf_tensor(PFX + "dnf", [1, 512], F32))
        rcr = e(nc.sbuf_tensor(PFX + "rcr", [1, 512], F32))
        rcc = e(nc.sbuf_tensor(PFX + "rcc", [128, 4], F32))
        onesc = e(nc.sbuf_tensor(PFX + "onesc", [128, 1], BF16))
        idsb = e(nc.sbuf_tensor(PFX + "idsb", [128, 128], F32))
        idb = e(nc.sbuf_tensor(PFX + "idb", [128, 128], BF16))
        st_ps = [e(nc.psum_tensor(PFX + f"st_ps{i}", [128, 512], F32)) for i in range(2)]
        av_ps = e(nc.psum_tensor(PFX + "av_ps", [128, 512], F32))
        dn_ps = e(nc.psum_tensor(PFX + "dn_ps", [1, 512], F32))
        at_ps = [e(nc.psum_tensor(PFX + f"at_ps{i}", [128, 128], F32)) for i in range(2)]
        rc_ps = e(nc.psum_tensor(PFX + "rc_ps", [128, 4], F32))
        s_c = es(nc.semaphore(PFX + "s_c"))
        s_qh = es(nc.semaphore(PFX + "s_qh"))
        s_sc = es(nc.semaphore(PFX + "s_sc"))
        s_ex = es(nc.semaphore(PFX + "s_ex"))
        s_av = es(nc.semaphore(PFX + "s_av"))
        s_ab = es(nc.semaphore(PFX + "s_ab"))
        s_rc = es(nc.semaphore(PFX + "s_rc"))
        s_rt = es(nc.semaphore(PFX + "s_rt"))
        s_rcc = es(nc.semaphore(PFX + "s_rcc"))
        s_at = es(nc.semaphore(PFX + "s_at"))
        s_ev2 = es(nc.semaphore(PFX + "s_ev2"))
        s_avn = es(nc.semaphore(PFX + "s_avn"))
        sst = es(nc.semaphore(PFX + "sst"))
        svs = es(nc.semaphore(PFX + "svs"))
        
        @block.sync
        def _(sync):
            if PH:
                sync.wait_ge(SD, PH)
            sync.dma_start(out=idsb[:, :], in_=idin).then_inc(s_c, 16)
            if ctx.get("kt_ag"):
                for r in range(4):
                    ktsrc = bass.AP(ktf.tensor, r * 1024 * 1024,
                                    [[1024, 128], [128 * 1024, 8], [1, 1024]])
                    dstv = ktb[:, :].rearrange("p (a b c) -> p a b c", a=8, b=4)[:, :, r, :]
                    sync.dma_start(out=dstv, in_=ktsrc).then_inc(s_c, 16)
            else:
                sync.dma_start(out=ktb[:, :].rearrange("p (a b) -> p a b", a=8),
                               in_=ktf.rearrange("(a p) b -> p a b", p=128)).then_inc(s_c, 16)
            sync.dma_start(out=vtb[:, :].rearrange("p (a b) -> p a b", a=SKT),
                           in_=vtf.rearrange("(a p) b -> p a b", p=128)).then_inc(s_c, 16)
            for h in range(H):
                if h >= 2:
                    sync.wait_ge(s_av, (h - 1) * 2)
                sync.dma_start(out=qth[h % 2][:, :],
                               in_=qTd[h * 128:(h + 1) * 128, :]).then_inc(s_qh, 16)
            sync.wait_ge(sst, H * 2 * 16)
            sync.nop().then_inc(SD, 1)

        @block.gpsimd
        def _(gp):
            if PH:
                gp.wait_ge(SD, PH)
            for i in range(H * 2):
                h, qc = divmod(i, 2)
                gp.wait_ge(s_avn, i + 1)
                dstap = bass.AP(attn.tensor, (qc * 4) * 128 * D + h * 128,
                                [[D, 128], [128 * D, 4], [1, 128]])
                gp.dma_start(out=dstap,
                             in_=avn[:, :].rearrange("p (a b) -> p a b", a=4)).then_inc(sst, 16)

        @block.vector
        def _(vec):
            if PH:
                vec.wait_ge(SD, PH)
            ser = Ser(vec, svs)
            vec.wait_ge(s_c, 16)
            vec.memset(onesc[:, :], 1.0)
            vec.tensor_copy(idb[:, :], idsb[:, :]).then_inc(s_c, 1)
            for i in range(H * 2):
                vec.wait_ge(s_av, i + 1)
                ser(vec.tensor_copy(dnf[:, :], dn_ps[:, :]))
                vec.reciprocal(rcr[:, :], dnf[:, :]).then_inc(s_rc, 1)
                vec.wait_ge(s_rt, (i + 1) * 4)
                vec.tensor_copy(rcc[:, :], rc_ps[:, :]).then_inc(s_rcc, 1)
                for qt in range(4):
                    g = i * 4 + qt
                    vec.wait_ge(s_at, g + 1)
                    if i >= 1 and qt == 0:
                        vec.wait_ge(sst, i * 16)
                    vec.tensor_scalar(avn[:, qt * 128:(qt + 1) * 128],
                                      at_ps[g % 2][:, :], rcc[:, qt:qt + 1], None,
                                      OP.mult).then_inc(s_ev2, 1)
                vec.nop().then_inc(s_avn, 1)

        @block.scalar
        def _(sc):
            if PH:
                sc.wait_ge(SD, PH)
            for i in range(H * 2):
                for kt in range(SKT):
                    g = i * SKT + kt
                    sc.wait_ge(s_sc, g + 1)
                    sc.activation(prb[:, kt * 512:(kt + 1) * 512], st_ps[g % 2][:, :],
                                  AF.Exp).then_inc(s_ex, 1)
                sc.wait_ge(s_av, i + 1)
                sc.activation(avb[:, :], av_ps[:, :], AF.Identity).then_inc(s_ab, 1)

        @block.tensor
        def _(te):
            if PH:
                te.wait_ge(SD, PH)
            te.wait_ge(s_c, 97 if ctx.get("kt_ag") else 49)
            for i in range(H * 2):
                h, qc = divmod(i, 2)
                kvh = h // 4
                te.wait_ge(s_qh, (h + 1) * 16)
                for kt in range(SKT):
                    g = i * SKT + kt
                    if g >= 2:
                        te.wait_ge(s_ex, g - 1)
                    te.matmul(st_ps[g % 2][:, :],
                              ktb[:, kvh * S + kt * 128:kvh * S + kt * 128 + 128],
                              qth[h % 2][:, qc * 512:(qc + 1) * 512],
                              start=True, stop=True).then_inc(s_sc, 1)
                te.wait_ge(s_ex, (i + 1) * SKT)
                if i >= 1:
                    te.wait_ge(s_ab, i)
                    te.wait_ge(s_rc, i)
                for kt in range(SKT):
                    te.matmul(av_ps[:, :], vtb[:, kt * 1024 + kvh * 128:kt * 1024 + kvh * 128 + 128],
                              prb[:, kt * 512:(kt + 1) * 512],
                              start=(kt == 0), stop=(kt == SKT - 1))
                for kt in range(SKT):
                    ins = te.matmul(dn_ps[:, :], onesc[:, 0:1],
                                    prb[:, kt * 512:(kt + 1) * 512],
                                    start=(kt == 0), stop=(kt == SKT - 1))
                ins.then_inc(s_av, 1)
                te.wait_ge(s_rc, i + 1)
                if i >= 1:
                    te.wait_ge(s_rcc, i)
                for j in range(4):
                    te.matmul(rc_ps[:, j:j + 1], rcr[0:1, j * 128:(j + 1) * 128],
                              idsb[0:1, 0:1], start=True, stop=True).then_inc(s_rt, 1)
                te.wait_ge(s_ab, i + 1)
                for qt in range(4):
                    g = i * 4 + qt
                    if g >= 2:
                        te.wait_ge(s_ev2, g - 1)
                    te.matmul(at_ps[g % 2][:, :], avb[:, qt * 128:(qt + 1) * 128],
                              idb[:, :], start=True, stop=True).then_inc(s_at, 1)
    return nc



def emit_l1_ffn(ctx):
    nc = ctx["nc"]; block = ctx["block"]; es = ctx["es"]
    PH = ctx["ph"]; SD = ctx["sd"]; PFX = ctx["prefix"]
    dram = ctx["dram"]
    xn1T = dram["xn1T"]
    qw2 = dram["qw2"]
    scl = dram["scl"]
    zro = dram["zro"]
    fbias = dram["fbias"]
    x1 = dram["x1"]
    atn = dram["atn"]
    out = dram["out"]

    with contextlib.ExitStack() as st:
        e = st.enter_context
        xn1a = e(nc.sbuf_tensor(PFX + "xn1a", [128, 32 * TOK], BF16))
        qw2s = e(nc.sbuf_tensor(PFX + "qw2s", [128, 4 * 512], I32))
        zexp = e(nc.sbuf_tensor(PFX + "zexp", [128, D], F32))
        sexp = e(nc.sbuf_tensor(PFX + "sexp", [128, D], F32))
        nibs = e(nc.sbuf_tensor(PFX + "nibs", [128, 4 * 512], I32))
        tmpw = e(nc.sbuf_tensor(PFX + "tmpw", [128, 512], F32))
        w2b = e(nc.sbuf_tensor(PFX + "w2b", [128, 4 * 512], BF16))
        fbb = e(nc.sbuf_tensor(PFX + "fbb", [1, D], BF16))
        onesc = e(nc.sbuf_tensor(PFX + "onesc", [1, 128], BF16))
        x1sl = [e(nc.sbuf_tensor(PFX + f"x1sl{i}", [128, 512], F32)) for i in range(2)]
        atsl = [e(nc.sbuf_tensor(PFX + f"atsl{i}", [128, 512], F32)) for i in range(2)]
        osl = [e(nc.sbuf_tensor(PFX + f"osl{i}", [128, 512], F32)) for i in range(2)]
        f_ps = [e(nc.psum_tensor(PFX + f"f_ps{i}", [128, 512], F32)) for i in range(8)]
        s_c = es(nc.semaphore(PFX + "s_c"))
        s_xa = es(nc.semaphore(PFX + "s_xa"))
        s_q2 = es(nc.semaphore(PFX + "s_q2"))
        s_dq2 = es(nc.semaphore(PFX + "s_dq2"))
        s_fmk = es(nc.semaphore(PFX + "s_fmk"))
        s_om2 = es(nc.semaphore(PFX + "s_om2"))
        s_xl = es(nc.semaphore(PFX + "s_xl"))
        s_fe = es(nc.semaphore(PFX + "s_fe"))
        sst = es(nc.semaphore(PFX + "sst"))
        svs = es(nc.semaphore(PFX + "svs"))
        
        @block.sync
        def _(sync):
            if PH:
                sync.wait_ge(SD, PH)
            sync.dma_start(out=zexp[:, :], in_=zro.to_broadcast([128, D])).then_inc(s_c, 16)
            sync.dma_start(out=sexp[:, :], in_=scl.to_broadcast([128, D])).then_inc(s_c, 16)
            for k in range(8):
                srcap = bass.AP(xn1T.tensor, k * TOK,
                                [[8 * TOK, 128], [8 * 128 * TOK, 4], [1, TOK]])
                sync.dma_start(out=xn1a[:, k * 4 * TOK:(k + 1) * 4 * TOK].rearrange(
                    "p (a b) -> p a b", a=4), in_=srcap).then_inc(s_xa, 16)
            for oc in range(8):
                if oc >= 1:
                    sync.wait_ge(s_dq2, oc * 8)
                srcap = bass.AP(qw2.tensor, oc * 512,
                                [[D, 128], [128 * D, 4], [1, 512]])
                sync.dma_start(out=qw2s[:, :].rearrange("p (a b) -> p a b", a=4),
                               in_=srcap).then_inc(s_q2, 16)
            sync.wait_ge(sst, 64 * 16)
            sync.nop().then_inc(SD, 1)

        @block.gpsimd
        def _(gp):
            if PH:
                gp.wait_ge(SD, PH)
            gp.dma_start(out=fbb[:, :], in_=fbias).then_inc(s_c, 16)
            for oc in range(8):
                for tc in range(8):
                    j = oc * 8 + tc
                    r0 = tc * 128
                    if j >= 2:
                        gp.wait_ge(s_fe, j - 1)
                    gp.dma_start(out=x1sl[j % 2][:, :],
                                 in_=x1[r0:r0 + 128, oc * 512:(oc + 1) * 512]).then_inc(s_xl, 16)
                    gp.dma_start(out=atsl[j % 2][:, :],
                                 in_=atn[r0:r0 + 128, oc * 512:(oc + 1) * 512]).then_inc(s_xl, 16)
                    gp.wait_ge(s_fe, j + 1)
                    gp.dma_start(out=out[r0:r0 + 128, oc * 512:(oc + 1) * 512],
                                 in_=osl[j % 2][:, :]).then_inc(sst, 16)

        @block.vector
        def _(vec):
            if PH:
                vec.wait_ge(SD, PH)
            ser = Ser(vec, svs)
            vec.wait_ge(s_c, 32)
            vec.memset(onesc[:, :], 1.0)
            for oc in range(8):
                vec.wait_ge(s_q2, (oc + 1) * 16)
                for k in range(8):
                    kk = oc * 8 + k
                    if kk >= 1:
                        vec.wait_ge(s_fmk, kk)
                    ser(vec.tensor_scalar(nibs[:, :], qw2s[:, :], 4 * k, 15,
                                          OP.logical_shift_right, OP.bitwise_and))
                    for rt in range(4):
                        sl = slice(rt * 512, (rt + 1) * 512)
                        ser(vec.tensor_tensor(tmpw[:, :], nibs[:, sl],
                                              zexp[:, oc * 512:(oc + 1) * 512], OP.subtract))
                        ser(vec.tensor_tensor(w2b[:, sl], tmpw[:, :],
                                              sexp[:, oc * 512:(oc + 1) * 512], OP.mult))
                    vec.nop().then_inc(s_dq2, 1)
                for tc in range(8):
                    j = oc * 8 + tc
                    vec.wait_ge(s_om2, j + 1)
                    vec.wait_ge(s_xl, (j + 1) * 32)
                    if j >= 2:
                        vec.wait_ge(sst, (j - 1) * 16)
                    ser(vec.tensor_tensor(osl[j % 2][:, :], f_ps[tc][:, :],
                                          x1sl[j % 2][:, :], OP.add))
                    vec.tensor_tensor(osl[j % 2][:, :], osl[j % 2][:, :],
                                      atsl[j % 2][:, :], OP.add).then_inc(s_fe, 1)

        @block.tensor
        def _(te):
            if PH:
                te.wait_ge(SD, PH)
            te.wait_ge(s_c, 48)
            te.wait_ge(s_xa, 8 * 16)
            for oc in range(8):
                if oc >= 1:
                    te.wait_ge(s_fe, oc * 8)
                for k in range(8):
                    kk = oc * 8 + k
                    te.wait_ge(s_dq2, kk + 1)
                    for rt in range(4):
                        for tc in range(8):
                            ins = te.matmul(
                                f_ps[tc][:, :],
                                xn1a[:, (k * 4 + rt) * TOK + tc * 128:(k * 4 + rt) * TOK + tc * 128 + 128],
                                w2b[:, rt * 512:(rt + 1) * 512],
                                start=(k == 0 and rt == 0), stop=False)
                    ins.then_inc(s_fmk, 1)
                for tc in range(8):
                    te.matmul(f_ps[tc][:, :], onesc[0:1, :],
                              fbb[0:1, oc * 512:(oc + 1) * 512],
                              start=False, stop=True).then_inc(s_om2, 1)
    return nc



_SPECS = {
    "ln_t": (emit_ln_t,
             [("x", [TOK, D], F32, "ExternalInput"),
              ("idin", [128, 128], F32, "ExternalInput"),
              ("xnT", [D, TOK], BF16, "ExternalOutput")]),
    "l0_attn": (emit_l0_attn,
                [("xnT", [D, TOK], BF16, "ExternalInput"),
                 ("w", [D, D], F32, "ExternalInput"),
                 ("mask", [D, D], I32, "ExternalInput"),
                 ("bias", [1, D], F32, "ExternalInput"),
                 ("x", [TOK, D], F32, "ExternalInput"),
                 ("et", [128, 252], BF16, "ExternalInput"),
                 ("idin", [128, 128], F32, "ExternalInput"),
                 ("y", [TOK, D], F32, "ExternalOutput"),
                 ("bmn_d", [128, D], BF16, "Internal")]),
    "l0_ffn": (emit_l0_ffn,
               [("xnT", [D, TOK], BF16, "ExternalInput"),
                ("qwg", [D, HFF], I32, "ExternalInput"),
                ("zg", [32, HFF], I32, "ExternalInput"),
                ("sg", [32, HFF], F32, "ExternalInput"),
                ("qwu", [D, HFF], I32, "ExternalInput"),
                ("zu", [32, HFF], I32, "ExternalInput"),
                ("su", [32, HFF], F32, "ExternalInput"),
                ("qwd", [HFF, D], I32, "ExternalInput"),
                ("zd", [86, D], I32, "ExternalInput"),
                ("sd_", [86, D], F32, "ExternalInput"),
                ("yy", [TOK, D], F32, "ExternalInput"),
                ("icb", [128, 63], BF16, "ExternalInput"),
                ("idbnd", [128, 171], BF16, "ExternalInput"),
                ("x1", [TOK, D], F32, "ExternalOutput"),
                ("sgb_d", [32, HFF], BF16, "Internal"),
                ("sub_d", [32, HFF], BF16, "Internal"),
                ("sdb_d", [86, D], BF16, "Internal"),
                ("nzg_d", [32, HFF], BF16, "Internal"),
                ("nzu_d", [32, HFF], BF16, "Internal"),
                ("nzd_d", [86, D], BF16, "Internal"),
                ("hgt_d", [128, NHT * 512], BF16, "Internal")]),
    "qkv": (emit_qkv,
            [("x", [TOK, D], F32, "ExternalInput"),
             ("wq", [D, D], F32, "ExternalInput"),
             ("bq", [1, D], F32, "ExternalInput"),
             ("wk", [D, 1024], F32, "ExternalInput"),
             ("bk", [1, 1024], F32, "ExternalInput"),
             ("wv", [D, 1024], F32, "ExternalInput"),
             ("bv", [1, 1024], F32, "ExternalInput"),
             ("cq", [64, TOK], F32, "ExternalInput"),
             ("sq_", [64, TOK], F32, "ExternalInput"),
             ("ck", [64, TOK], F32, "ExternalInput"),
             ("sk_", [64, TOK], F32, "ExternalInput"),
             ("idin", [128, 128], F32, "ExternalInput"),
             ("xn1T", [D, TOK], BF16, "ExternalOutput"),
             ("qT", [D, TOK], BF16, "ExternalOutput"),
             ("ksh", [1024, TOK], BF16, "ExternalOutput"),
             ("vsh", [TOK, 1024], BF16, "ExternalOutput")]),
    "attn": (emit_attn,
             [("qTd", [D, TOK], BF16, "ExternalInput"),
              ("ktf", [1024, S], BF16, "ExternalInput"),
              ("vtf", [S, 1024], BF16, "ExternalInput"),
              ("idin", [128, 128], F32, "ExternalInput"),
              ("attn", [TOK, D], F32, "ExternalOutput")]),
    "l1_ffn": (emit_l1_ffn,
               [("xn1T", [D, TOK], BF16, "ExternalInput"),
                ("qw2", [512, D], I32, "ExternalInput"),
                ("scl", [1, D], F32, "ExternalInput"),
                ("zro", [1, D], F32, "ExternalInput"),
                ("fbias", [1, D], F32, "ExternalInput"),
                ("x1", [TOK, D], F32, "ExternalInput"),
                ("atn", [TOK, D], F32, "ExternalInput"),
                ("out", [TOK, D], F32, "ExternalOutput")]),
}


def _build_standalone(key):
    emit, specs = _SPECS[key]
    nc = bass.Bass("TRN2", target_bir_lowering=False)
    dram = {}
    for nm, shape, ty, kind in specs:
        h = nc.dram_tensor(nm, shape, ty, kind=kind)
        dram[nm] = h if kind == "Internal" else h.ap()
    with contextlib.ExitStack() as st0:
        es = st0.enter_context
        sdn = es(nc.semaphore("sdn"))
        block = es(nc.Block())
        emit(_mkctx(nc, block, es, dram, sdn, 0, "s_"))
    return nc


def build_ln_t():
    return _build_standalone("ln_t")


def build_l0_attn(debug=False):
    return _build_standalone("l0_attn")


def build_l0_ffn():
    return _build_standalone("l0_ffn")


def build_qkv():
    return _build_standalone("qkv")


def build_attn():
    return _build_standalone("attn")


def build_l1_ffn():
    return _build_standalone("l1_ffn")



def emit_cc(ctx, ksh_d, vsh_d, ktf_d, vtf_d):
    nc = ctx["nc"]; block = ctx["block"]; es = ctx["es"]
    PH = ctx["ph"]; SD = ctx["sd"]
    cc = es(nc.semaphore(ctx["prefix"] + "cc"))

    @block.gpsimd
    def _(gp):
        gp.wait_ge(SD, PH)
        rg = [[0, 1, 2, 3], [4, 5, 6, 7]]
        gp.collective_compute("AllGather", OP.bypass, replica_groups=rg,
                              ins=[ksh_d.ap().opt()],
                              outs=[ktf_d.ap().opt()]).then_inc(cc, 1)
        gp.collective_compute("AllGather", OP.bypass, replica_groups=rg,
                              ins=[vsh_d.ap().opt()],
                              outs=[vtf_d.ap().opt()]).then_inc(cc, 1)
        gp.wait_ge(cc, 2)
        gp.nop().then_inc(SD, 1)


def build_fused():
    nc = bass.Bass("TRN2", target_bir_lowering=False)
    EI = "ExternalInput"

    def ein(nm, shape, ty):
        return nc.dram_tensor(nm, shape, ty, kind=EI).ap()

    x_in = ein("x", [TOK, D], F32)
    w0 = ein("w0", [D, D], F32)
    mask0 = ein("mask0", [D, D], I32)
    b0 = ein("b0", [1, D], F32)
    qwg = ein("qwg", [D, HFF], I32)
    zg = ein("zg", [32, HFF], I32)
    sg = ein("sg", [32, HFF], F32)
    qwu = ein("qwu", [D, HFF], I32)
    zu = ein("zu", [32, HFF], I32)
    su = ein("su", [32, HFF], F32)
    qwd = ein("qwd", [HFF, D], I32)
    zd = ein("zd", [86, D], I32)
    sdn = ein("sdn", [86, D], F32)
    wq = ein("wq", [D, D], F32)
    bq = ein("bq", [1, D], F32)
    wk = ein("wk", [D, 1024], F32)
    bk = ein("bk", [1, 1024], F32)
    wv = ein("wv", [D, 1024], F32)
    bv = ein("bv", [1, 1024], F32)
    qw2 = ein("qw2", [512, D], I32)
    scl = ein("scl", [1, D], F32)
    zro = ein("zro", [1, D], F32)
    fb2 = ein("fb2", [1, D], F32)
    idin = ein("idin", [128, 128], F32)
    et = ein("et", [128, 252], BF16)
    icb = ein("icb", [128, 63], BF16)
    idbnd = ein("idbnd", [128, 171], BF16)
    cq = ein("cq", [64, TOK], F32)
    sq_ = ein("sq_", [64, TOK], F32)
    ck = ein("ck", [64, TOK], F32)
    sk_ = ein("sk_", [64, TOK], F32)
    out = nc.dram_tensor("out", [TOK, D], F32, kind="ExternalOutput").ap()
    xnT_d = nc.dram_tensor("xnT_d", [D, TOK], BF16)
    y_d = nc.dram_tensor("y_d", [TOK, D], F32)
    x1_d = nc.dram_tensor("x1_d", [TOK, D], F32)
    xn1T_d = nc.dram_tensor("xn1T_d", [D, TOK], BF16)
    qT_d = nc.dram_tensor("qT_d", [D, TOK], BF16)
    ksh_d = nc.dram_tensor("ksh_d", [1024, TOK], BF16)
    vsh_d = nc.dram_tensor("vsh_d", [TOK, 1024], BF16)
    ktf_d = nc.dram_tensor("ktf_d", [4096, 1024], BF16)
    vtf_d = nc.dram_tensor("vtf_d", [S, 1024], BF16)
    attn_d = nc.dram_tensor("attn_d", [TOK, D], F32)
    bmn_d = nc.dram_tensor("bmn_d", [128, D], BF16)
    sgb_d = nc.dram_tensor("sgb_d", [32, HFF], BF16)
    sub_d = nc.dram_tensor("sub_d", [32, HFF], BF16)
    sdb_d = nc.dram_tensor("sdb_d", [86, D], BF16)
    nzg_d = nc.dram_tensor("nzg_d", [32, HFF], BF16)
    nzu_d = nc.dram_tensor("nzu_d", [32, HFF], BF16)
    nzd_d = nc.dram_tensor("nzd_d", [86, D], BF16)
    hgt_d = nc.dram_tensor("hgt_d", [128, NHT * 512], BF16)

    with contextlib.ExitStack() as st0:
        es = st0.enter_context
        sdone = es(nc.semaphore("sdone"))
        block = es(nc.Block())
        emit_ln_t(_mkctx(nc, block, es, dict(
            x=x_in, idin=idin, xnT=xnT_d.ap()), sdone, 0, "pa_"))
        emit_l0_attn(_mkctx(nc, block, es, dict(
            xnT=xnT_d.ap(), w=w0, mask=mask0, bias=b0, x=x_in, et=et,
            idin=idin, y=y_d.ap(), bmn_d=bmn_d), sdone, 1, "pb_"))
        emit_l0_ffn(_mkctx(nc, block, es, dict(
            xnT=xnT_d.ap(), qwg=qwg, zg=zg, sg=sg, qwu=qwu, zu=zu, su=su,
            qwd=qwd, zd=zd, sd_=sdn, yy=y_d.ap(), icb=icb, idbnd=idbnd,
            x1=x1_d.ap(), sgb_d=sgb_d, sub_d=sub_d, sdb_d=sdb_d,
            nzg_d=nzg_d, nzu_d=nzu_d, nzd_d=nzd_d, hgt_d=hgt_d),
            sdone, 2, "pc_"))
        emit_qkv(_mkctx(nc, block, es, dict(
            x=x1_d.ap(), wq=wq, bq=bq, wk=wk, bk=bk, wv=wv, bv=bv,
            cq=cq, sq_=sq_, ck=ck, sk_=sk_, idin=idin,
            xn1T=xn1T_d.ap(), qT=qT_d.ap(), ksh=ksh_d.ap(), vsh=vsh_d.ap()),
            sdone, 3, "pd_"))
        emit_cc(_mkctx(nc, block, es, {}, sdone, 4, "pg_"),
                ksh_d, vsh_d, ktf_d, vtf_d)
        ctxe = _mkctx(nc, block, es, dict(
            qTd=qT_d.ap(), ktf=ktf_d.ap(), vtf=vtf_d.ap(), idin=idin,
            attn=attn_d.ap()), sdone, 5, "pe_")
        ctxe["kt_ag"] = True
        emit_attn(ctxe)
        emit_l1_ffn(_mkctx(nc, block, es, dict(
            xn1T=xn1T_d.ap(), qw2=qw2, scl=scl, zro=zro, fbias=fb2,
            x1=x1_d.ap(), atn=attn_d.ap(), out=out), sdone, 6, "pf_"))
    return nc



def _shard_rows(a):
    return [np.ascontiguousarray(a[c * TOK:(c + 1) * TOK]) for c in range(NCORES)]


def _consts():
    ident = np.eye(128, dtype=np.float32)
    p = np.arange(128)
    et = np.zeros((128, 252), dtype=np.float32)
    et[p, p // 32 + 124] = 1.0
    icb = np.zeros((128, 63), np.float32); icb[:, 31] = 1.0
    idbnd = np.zeros((128, 171), np.float32); idbnd[:, 85] = 1.0
    inv = 1.0 / 10000.0 ** (np.arange(0, 128, 2, dtype=np.float32) / 128)
    tabs = []
    for c in range(NCORES):
        pos = np.arange((c % 4) * TOK, (c % 4) * TOK + TOK, dtype=np.float32)
        ang = pos[:, None] * inv[None, :]
        tabs.append((np.ascontiguousarray(np.cos(ang).T.astype(np.float32)),
                     np.ascontiguousarray(np.sin(ang).T.astype(np.float32))))
    return ident, np.asarray(et.astype(BF)), np.asarray(icb.astype(BF)), \
        np.asarray(idbnd.astype(BF)), tabs


def _run(nc, maps):
    res = run_bass_kernel_spmd(nc, maps, core_ids=list(range(NCORES)))
    return res.results


def kernel(**inp):
    x0 = np.ascontiguousarray(np.asarray(inp["L0_attn_data"], dtype=np.float32)
                              .reshape(B * S, D))
    ident, et, icb, idbnd, tabs = _consts()
    xs = _shard_rows(x0)
    sc = np.float32(1.0 / np.sqrt(128.0))
    nc = build_fused()
    maps = []
    for c in range(NCORES):
        maps.append(dict(
            x=xs[c], w0=inp["L0_attn_weight"], mask0=inp["L0_attn_mask"],
            b0=inp["L0_attn_bias"].reshape(1, D),
            qwg=inp["L0_ffn_qw_gate"], zg=inp["L0_ffn_z_gate"],
            sg=inp["L0_ffn_s_gate"], qwu=inp["L0_ffn_qw_up"],
            zu=inp["L0_ffn_z_up"], su=inp["L0_ffn_s_up"],
            qwd=inp["L0_ffn_qw_down"], zd=inp["L0_ffn_z_down"],
            sdn=inp["L0_ffn_s_down"],
            wq=inp["L1_attn_wq"], bq=inp["L1_attn_bq"].reshape(1, D),
            wk=inp["L1_attn_wk"], bk=inp["L1_attn_bk"].reshape(1, 1024),
            wv=inp["L1_attn_wv"], bv=inp["L1_attn_bv"].reshape(1, 1024),
            qw2=inp["L1_ffn_quantized_weights"],
            scl=inp["L1_ffn_scales"].reshape(1, D),
            zro=inp["L1_ffn_zeros"].reshape(1, D),
            fb2=inp["L1_ffn_bias"].reshape(1, D),
            idin=ident, et=et, icb=icb, idbnd=idbnd,
            cq=tabs[c][0] * sc, sq_=tabs[c][1] * sc,
            ck=tabs[c][0], sk_=tabs[c][1]))
    res = _run(nc, maps)
    out = np.concatenate([r["out"] for r in res], axis=0)
    return out.reshape(B, S, D)
